# version 01
# baseline (speedup 1.0000x reference)
import numpy as np
from contextlib import ExitStack
import concourse.bass as bass
import concourse.mybir as mybir
from concourse.bass_utils import run_bass_kernel_spmd

F32 = mybir.dt.float32
BF16 = mybir.dt.bfloat16
I32 = mybir.dt.int32
AF = mybir.ActivationFunctionType
ALU = mybir.AluOpType
AXL = mybir.AxisListType

NCORES = 8
DM = 2048
DIN = 4096
DXB = 1024
DS = 16
DTR = 128
HID = 5632
NBATCH = 4
L = 4096
NTOK = NBATCH * L
NH = 16
HD = 128
ABS = 256
EPS_LN = 1e-5
ASCALE = 1.0 / float(np.sqrt(HD))
NDM = DM // 128


class P:
    ENG = ("sync", "scalar", "vector", "tensor", "gpsimd")

    def __init__(self, nc, st, sem_names=("sc", "ve", "te", "gp"),
                 dma_slots=("ld0", "ld1", "st0", "st1"), consts=()):
        self.nc = nc
        self.st = st
        self.ops = {e: [] for e in self.ENG}
        self.sem = {}
        self.cnt = {}
        for s in sem_names:
            self.sem[s] = st.enter_context(nc.semaphore(s))
            self.cnt[s] = 0
        for s in dma_slots:
            self.sem["d:" + s] = st.enter_context(nc.semaphore("d_" + s))
            self.cnt["d:" + s] = 0
        self._cvals = {}
        if consts:
            vals = sorted(set(float(v) for v in consts))
            self.cpool = st.enter_context(
                nc.sbuf_tensor("cpool", [128, len(vals)], F32))
            for j, v in enumerate(vals):
                self._cvals[v] = j
                fn = lambda e, j=j, v=v: e.memset(self.cpool[:, j:j + 1], v)
                if j == len(vals) - 1:
                    self.run("gpsimd", fn, inc="gp")
                else:
                    self.run("gpsimd", fn)
            for eng in ("scalar", "vector", "tensor"):
                self.wait_cur(eng, "gp")

    def const(self, v):
        return self.cpool[:, self._cvals[float(v)]:self._cvals[float(v)] + 1]

    def _app(self, eng, fn):
        self.ops[eng].append(fn)

    def wait(self, eng, sem, val):
        if val <= 0:
            return
        h = self.sem[sem]
        self._app(eng, lambda e, h=h, v=val: e.wait_ge(h, v))

    def wait_cur(self, eng, sem):
        self.wait(eng, sem, self.cnt[sem])

    OWN = {"scalar": "sc", "vector": "ve", "tensor": "te", "gpsimd": "gp"}

    def run(self, eng, fn, inc=None, inc_n=1, serialize=True):
        own = self.OWN.get(eng)
        if own is not None and eng != "tensor":
            if serialize and self.cnt[own] > 0:
                self.wait(eng, own, self.cnt[own])
            self.cnt[own] += 1
            v = self.cnt[own]
            h = self.sem[own]
            extra = None
            if inc is not None and inc != own:
                extra = (self.sem[inc], inc_n)
                self.cnt[inc] += inc_n

            def g(e, fn=fn, h=h, extra=extra):
                ins = fn(e)
                ins.then_inc(h, 1)
                if extra is not None:
                    ins.then_inc(*extra)

            self._app(eng, g)
            return v
        if inc is None:
            self._app(eng, fn)
            return None
        h = self.sem[inc]
        self.cnt[inc] += inc_n
        v = self.cnt[inc]

        def g(e, fn=fn, h=h, n=inc_n):
            fn(e).then_inc(h, n)

        self._app(eng, g)
        return v

    def dma(self, eng, out, in_, slot="ld0", **kw):
        sem = "d:" + slot
        h = self.sem[sem]
        self.cnt[sem] += 16
        v = self.cnt[sem]
        self._app(eng, lambda e, h=h: e.dma_start(out=out, in_=in_, **kw)
                  .then_inc(h, 16))
        return v

    def dwait(self, eng, slot):
        self.wait(eng, "d:" + slot, self.cnt["d:" + slot])

    def finalize(self):
        nc = self.nc
        ops = self.ops
        with nc.Block() as blk:
            @blk.sync
            def _(eng):
                for f in ops["sync"]:
                    f(eng)

            @blk.scalar
            def _(eng):
                for f in ops["scalar"]:
                    f(eng)

            @blk.vector
            def _(eng):
                for f in ops["vector"]:
                    f(eng)

            @blk.tensor
            def _(eng):
                for f in ops["tensor"]:
                    f(eng)

            @blk.gpsimd
            def _(eng):
                for f in ops["gpsimd"]:
                    f(eng)
        return nc


class LNBufs:

    def __init__(self, nc, st, tc, tag=""):
        sb = lambda name, shape, dt=F32: st.enter_context(
            nc.sbuf_tensor(name + tag, shape, dt))
        pt = lambda name, shape, dt=F32: st.enter_context(
            nc.psum_tensor(name + tag, shape, dt))
        self.tc = tc
        self.sq = [sb(f"ln_sq{j}", [128, tc]) for j in range(2)]
        self.scr = [sb(f"ln_scr{j}", [128, tc]) for j in range(2)]
        self.ones_inv = sb("ln_ones_inv", [128, 1])
        self.ones_bc = sb("ln_ones_bc", [1, 128])
        self.m = sb("ln_m", [1, tc])
        self.msq = sb("ln_msq", [1, tc])
        self.var = sb("ln_var", [1, tc])
        self.rstd = sb("ln_rstd", [1, tc])
        self.negm = sb("ln_negm", [1, tc])
        self.ps_s = pt("ln_ps_s", [128, tc])
        self.ps_s2 = pt("ln_ps_s2", [128, tc])
        self.ps_mb = pt("ln_ps_mb", [128, tc])
        self.ps_rb = pt("ln_ps_rb", [128, tc])

    def emit_init(self, p):
        p.run("gpsimd", lambda e: e.memset(self.ones_inv[:, :], 1.0 / DM))
        p.run("gpsimd", lambda e: e.memset(self.ones_bc[0:1, :], 1.0), inc="gp")
        p.wait_cur("tensor", "gp")


def emit_ln_chunk(p, ln, ndm, in_tile, out_tile, out_done):
    tc = ln.tc
    sc0 = p.cnt["sc"]
    te0 = p.cnt["te"]
    p.wait("scalar", "te", te0 - 1)
    for i in range(ndm):
        it = in_tile(i)
        if i >= 2:
            p.wait("scalar", "te", te0 + i - 1)
        p.run("scalar",
              lambda e, it=it, i=i: e.activation(out=ln.sq[i % 2][:, :], in_=it,
                                                 func=AF.Square),
              inc="sc")
        p.run("tensor",
              lambda e, it=it, i=i: e.matmul(ln.ps_s[0:1, :], ln.ones_inv[:, 0:1],
                                             it, start=(i == 0),
                                             stop=(i == ndm - 1)))
        p.wait("tensor", "sc", sc0 + i + 1)
        p.run("tensor",
              lambda e, i=i: e.matmul(ln.ps_s2[0:1, :], ln.ones_inv[:, 0:1],
                                      ln.sq[i % 2][:, :], start=(i == 0),
                                      stop=(i == ndm - 1)),
              inc="te")
    p.wait_cur("vector", "te")
    p.run("vector", lambda e: e.tensor_copy(out=ln.m[0:1, :], in_=ln.ps_s[0:1, :]))
    p.run("vector", lambda e: e.tensor_tensor(out=ln.msq[0:1, :], in0=ln.m[0:1, :],
                                              in1=ln.m[0:1, :], op=ALU.mult))
    p.run("vector", lambda e: e.tensor_tensor(out=ln.var[0:1, :],
                                              in0=ln.ps_s2[0:1, :],
                                              in1=ln.msq[0:1, :],
                                              op=ALU.subtract))
    p.run("vector", lambda e: e.tensor_scalar_mul(ln.negm[0:1, :], ln.m[0:1, :],
                                                  -1.0), inc="ve")
    p.wait_cur("scalar", "ve")
    p.run("scalar", lambda e: e.activation(out=ln.msq[0:1, :], in_=ln.var[0:1, :],
                                           func=AF.Ln,
                                           bias=p.const(EPS_LN)[0:1, :]))
    p.run("scalar", lambda e: e.activation(out=ln.rstd[0:1, :], in_=ln.msq[0:1, :],
                                           func=AF.Exp,
                                           scale=p.const(-0.5)[0:1, :]),
          inc="sc")
    p.wait_cur("tensor", "sc")
    p.run("tensor", lambda e: e.matmul(ln.ps_mb[:, :], ln.ones_bc[0:1, :],
                                       ln.negm[0:1, :], start=True, stop=True))
    p.run("tensor", lambda e: e.matmul(ln.ps_rb[:, :], ln.ones_bc[0:1, :],
                                       ln.rstd[0:1, :], start=True, stop=True),
          inc="te")
    p.wait_cur("vector", "te")
    for i in range(ndm):
        it = in_tile(i)
        p.run("vector", lambda e, it=it, i=i: e.tensor_tensor(
            out=ln.scr[i % 2][:, :], in0=it, in1=ln.ps_mb[:, :], op=ALU.add))
        ot = out_tile(i)
        v = p.run("vector", lambda e, ot=ot, i=i: e.tensor_tensor(
            out=ot, in0=ln.scr[i % 2][:, :], in1=ln.ps_rb[:, :],
            op=ALU.mult), inc="ve")
        out_done(i, v)


class TMLNBufs:

    def __init__(self, nc, st, width, tag=""):
        sb = lambda name, shape, dt=F32: st.enter_context(
            nc.sbuf_tensor(name + tag, shape, dt))
        self.width = width
        self.sq = sb("tmln_sq", [128, width])
        self.s1 = sb("tmln_s1", [128, 1])
        self.s2 = sb("tmln_s2", [128, 1])
        self.m = sb("tmln_m", [128, 1])
        self.msq = sb("tmln_msq", [128, 1])
        self.var = sb("tmln_var", [128, 1])
        self.rstd = sb("tmln_rstd", [128, 1])
        self.negm = sb("tmln_negm", [128, 1])


def emit_ln_tm(p, ln, in_ap, out_ap):
    w = float(ln.width)
    p.run("vector", lambda e: e.scalar_tensor_tensor(
        out=ln.sq[:, :], in0=in_ap, scalar=1.0, in1=in_ap,
        op0=ALU.mult, op1=ALU.mult, accum_out=ln.s2[:, :]))
    p.run("vector", lambda e: e.tensor_reduce(
        out=ln.s1[:, :], in_=in_ap, axis=AXL.X, op=ALU.add))
    p.run("vector", lambda e: e.tensor_scalar_mul(ln.m[:, :], ln.s1[:, :],
                                                  1.0 / w))
    p.run("vector", lambda e: e.tensor_tensor(out=ln.msq[:, :], in0=ln.m[:, :],
                                              in1=ln.m[:, :], op=ALU.mult))
    p.run("vector", lambda e: e.scalar_tensor_tensor(
        out=ln.var[:, :], in0=ln.s2[:, :], scalar=1.0 / w, in1=ln.msq[:, :],
        op0=ALU.mult, op1=ALU.subtract))
    p.run("vector", lambda e: e.tensor_scalar_mul(ln.negm[:, :], ln.m[:, :],
                                                  -1.0), inc="ve")
    p.wait_cur("scalar", "ve")
    p.run("scalar", lambda e: e.activation(out=ln.msq[:, :], in_=ln.var[:, :],
                                           func=AF.Ln, bias=p.const(EPS_LN)))
    p.run("scalar", lambda e: e.activation(out=ln.rstd[:, :], in_=ln.msq[:, :],
                                           func=AF.Exp, scale=p.const(-0.5)),
          inc="sc")
    p.wait_cur("vector", "sc")
    return p.run("vector", lambda e: e.tensor_scalar(
        out=out_ap, in0=in_ap, scalar1=ln.negm[:, :], scalar2=ln.rstd[:, :],
        op0=ALU.add, op1=ALU.mult), inc="ve")


def build_ln0(tok_per):
    ntiles = tok_per // 128
    nc = bass.Bass("TRN2", target_bir_lowering=False, debug=False)
    hs = nc.dram_tensor("hs", [tok_per, DM], F32, kind="ExternalInput").ap()
    xn = nc.dram_tensor("xn", [tok_per, DM], BF16, kind="ExternalOutput").ap()
    st = ExitStack()
    with st:
        p = P(nc, st, consts=[EPS_LN, -0.5])
        sb = lambda name, shape, dt=F32: st.enter_context(
            nc.sbuf_tensor(name, shape, dt))
        hst = [sb(f"hst{j}", [128, DM]) for j in range(2)]
        xnt = [sb(f"xnt{j}", [128, DM], BF16) for j in range(2)]
        ln = TMLNBufs(nc, st, DM)
        done_ve = [0, 0]
        for t in range(ntiles):
            buf = t % 2
            p.wait("sync", "ve", done_ve[buf])
            p.dma("sync", hst[buf][:, :], hs[t * 128:(t + 1) * 128, :],
                  slot=f"ld{buf}")
            p.dwait("vector", f"ld{buf}")
            p.dwait("vector", f"st{buf}")
            v = emit_ln_tm(p, ln, hst[buf][:, :], xnt[buf][:, :])
            done_ve[buf] = v
            p.wait("gpsimd", "ve", v)
            p.dma("gpsimd", xn[t * 128:(t + 1) * 128, :], xnt[buf][:, :],
                  slot=f"st{buf}")
        p.dwait("gpsimd", "st0")
        p.dwait("gpsimd", "st1")
    return p.finalize()


NCT = 4
TC = 512
SC2 = 2
TSC = TC // SC2


def emit_load_wbf(p, dram, wbf, ktiles, fcols, stage, castv, slot_base="w"):
    for k in range(ktiles):
        j = k % 2
        p.wait("sync", "ve", castv[j])
        p.dma("sync", stage[j][:, 0:fcols], dram[k * 128:(k + 1) * 128, :],
              slot=slot_base + str(j))
        p.dwait("vector", slot_base + str(j))
        castv[j] = p.run("vector", lambda e, k=k, j=j: e.tensor_copy(
            out=wbf[:, k * fcols:(k + 1) * fcols], in_=stage[j][:, 0:fcols]))
    return castv


def build_l0_mamba(nbatch=NBATCH, ltok=L):
    nc = bass.Bass("TRN2", target_bir_lowering=False, debug=False)
    ntok = nbatch * ltok
    dram = lambda name, shape, dt, kind: nc.dram_tensor(name, shape, dt,
                                                        kind=kind).ap()
    xn = dram("xn", [ntok, DM], BF16, "ExternalInput")
    wxT = dram("wxT", [DM, 128], F32, "ExternalInput")
    wbT = dram("wbT", [DM, 128], F32, "ExternalInput")
    wzT = dram("wzT", [DM, 512], F32, "ExternalInput")
    wcT = dram("wcT", [DM, 512], F32, "ExternalInput")
    wdtdT = dram("wdtdT", [DM, 128], F32, "ExternalInput")
    wdtT = dram("wdtT", [128, 512], F32, "ExternalInput")
    bx_s = dram("bx_s", [128, 1], F32, "ExternalInput")
    bz_s = dram("bz_s", [128, NCT], F32, "ExternalInput")
    bdt_s = dram("bdt_s", [128, NCT], F32, "ExternalInput")
    alog_s = dram("alog_s", [128, NCT * DS], F32, "ExternalInput")
    d_s = dram("d_s", [128, NCT], F32, "ExternalInput")
    convw_s = dram("convw_s", [128, NCT * 4], F32, "ExternalInput")
    convb_s = dram("convb_s", [128, NCT], F32, "ExternalInput")
    selx = dram("selx", [128, NCT * 128], BF16, "ExternalInput")
    selB = dram("selB", [128, NCT * DS * 128], BF16, "ExternalInput")
    selC = dram("selC", [128, DS * 128], BF16, "ExternalInput")
    yg = dram("yg", [512, ntok], BF16, "ExternalOutput")

    st = ExitStack()
    with st:
        p = P(nc, st, dma_slots=("w0", "w1", "x0", "x1", "yg0", "yg1", "cst"),
              consts=[-1.0, 1.0])
        sb = lambda name, shape, dt=F32: st.enter_context(
            nc.sbuf_tensor(name, shape, dt))
        pt = lambda name, shape, dt=F32: st.enter_context(
            nc.psum_tensor(name, shape, dt))

        wx_bf = sb("wx_bf", [128, 16 * 128], BF16)
        wb_bf = sb("wb_bf", [128, 16 * 128], BF16)
        wz_bf = sb("wz_bf", [128, 16 * 512], BF16)
        wc_bf = sb("wc_bf", [128, 16 * 512], BF16)
        wdtd_bf = sb("wdtd_bf", [128, 16 * 128], BF16)
        wdt_bf = sb("wdt_bf", [128, 512], BF16)
        selx_bf = sb("selx_bf", [128, NCT * 128], BF16)
        selB_bf = sb("selB_bf", [128, NCT * DS * 128], BF16)
        selC_bf = sb("selC_bf", [128, DS * 128], BF16)
        bx_sb = sb("bx_sb", [128, 1])
        bz_sb = sb("bz_sb", [128, NCT])
        bdt2_sb = sb("bdt2_sb", [128, NCT])
        A_sb = sb("A_sb", [128, NCT * DS])
        D_sb = sb("D_sb", [128, NCT])
        convw_sb = sb("convw_sb", [128, NCT * 4])
        convb_sb = sb("convb_sb", [128, NCT])
        stage = [sb(f"wstage{j}", [128, 512]) for j in range(2)]
        xnf = sb("xnf", [128, 16 * TC], BF16)
        x_bf = sb("x_bf", [128, TC], BF16)
        B_bf = sb("B_bf", [128, TC], BF16)
        dtd_bf = sb("dtd_bf", [128, TC], BF16)
        sz_sb = sb("sz_sb", [128, NCT * TC])
        C_bf = sb("C_bf", [128, NCT * TC], BF16)
        delta_sb = sb("delta_sb", [128, NCT * TC])
        u_sb = sb("u_sb", [128, NCT * TC])
        du_sb = sb("du_sb", [128, TC])
        xrep = [sb(f"xrep{ct}", [128, 3 + TC]) for ct in range(NCT)]
        tail = [sb(f"tail{ct}", [128, 3]) for ct in range(NCT)]
        relu_sb = sb("relu_sb", [128, TC])
        t1_sb = sb("t1_sb", [128, TC])
        t2_sb = sb("t2_sb", [128, TC])
        acc_sb = [sb(f"acc{j}", [128, TC]) for j in range(2)]
        a_buf = sb("a_buf", [128, DS * TSC])
        b_buf = sb("b_buf", [128, DS * TSC])
        h_buf = sb("h_buf", [128, DS * TSC])
        hstate = sb("hstate", [128, NCT * DS])
        y_sb = sb("y_sb", [128, TSC])
        yg_bf = [sb(f"yg_bf{j}", [128, TC], BF16) for j in range(2)]
        ps = [pt(f"ps{i}", [128, 512]) for i in range(6)]

        castv = [0, 0]
        emit_load_wbf(p, wxT, wx_bf, 16, 128, stage, castv)
        emit_load_wbf(p, wbT, wb_bf, 16, 128, stage, castv)
        emit_load_wbf(p, wzT, wz_bf, 16, 512, stage, castv)
        emit_load_wbf(p, wcT, wc_bf, 16, 512, stage, castv)
        emit_load_wbf(p, wdtdT, wdtd_bf, 16, 128, stage, castv)
        emit_load_wbf(p, wdtT, wdt_bf, 1, 512, stage, castv)
        for ap, sbuf in ((selx, selx_bf), (selB, selB_bf), (selC, selC_bf),
                         (bx_s, bx_sb), (bz_s, bz_sb), (d_s, D_sb),
                         (convw_s, convw_sb), (convb_s, convb_sb)):
            p.dma("sync", sbuf[:, :], ap, slot="cst")
        p.dma("sync", A_sb[:, :], alog_s, slot="cst")
        p.dma("sync", bdt2_sb[:, :], bdt_s, slot="cst")
        p.dwait("scalar", "cst")
        p.dwait("vector", "cst")
        vA = p.run("scalar", lambda e: e.activation(out=A_sb[:, :], in_=A_sb[:, :],
                                                    func=AF.Exp))
        p.wait("vector", "sc", vA)
        p.run("vector", lambda e: e.tensor_scalar_mul(A_sb[:, :], A_sb[:, :], -1.0))
        p.run("vector", lambda e: e.tensor_tensor(out=bdt2_sb[:, :],
                                                  in0=bdt2_sb[:, :],
                                                  in1=bdt2_sb[:, :], op=ALU.add))
        for ct in range(NCT):
            p.run("vector", lambda e, ct=ct: e.memset(tail[ct][:, :], 0.0))
        vinit = p.run("vector", lambda e: e.memset(hstate[:, :], 0.0))
        p.wait("tensor", "ve", vinit)
        p.wait("scalar", "ve", vinit)

        yg_dma = [0, 0]
        v_gate_last = 0
        v_u_last = 0
        psread = [0, 0]

        for b in range(nbatch):
            if b > 0:
                for ct in range(NCT):
                    p.run("vector", lambda e, ct=ct: e.memset(tail[ct][:, :], 0.0))
                p.run("vector", lambda e: e.memset(hstate[:, :], 0.0))
            for cc in range(ltok // TC):
                tok0 = b * ltok + cc * TC
                ld = cc % 2
                p.wait_cur("sync", "te")
                for k in range(16):
                    p.dma("sync", xnf[:, k * TC:(k + 1) * TC],
                          xn[tok0:tok0 + TC, k * 128:(k + 1) * 128],
                          slot=f"x{ld}", transpose=True)
                p.dwait("tensor", f"x{ld}")
                p.wait("tensor", "ve", p.cnt["ve"])
                p.wait("tensor", "sc", p.cnt["sc"])
                p.wait("scalar", "ve", v_gate_last)

                def rhs(k):
                    return xnf[:, k * TC:(k + 1) * TC]

                for k in range(16):
                    p.run("tensor", lambda e, k=k: e.matmul(
                        ps[0][:, :], wx_bf[:, k * 128:(k + 1) * 128], rhs(k),
                        start=(k == 0), stop=(k == 15)))
                    p.run("tensor", lambda e, k=k: e.matmul(
                        ps[1][:, :], wb_bf[:, k * 128:(k + 1) * 128], rhs(k),
                        start=(k == 0), stop=(k == 15)))
                    p.run("tensor", lambda e, k=k: e.matmul(
                        ps[2][:, :], wdtd_bf[:, k * 128:(k + 1) * 128], rhs(k),
                        start=(k == 0), stop=(k == 15)),
                        inc=("te" if k == 15 else None))
                v_xbd = p.cnt["te"]
                p.wait("scalar", "te", v_xbd)
                p.wait("vector", "te", v_xbd)
                v_x = p.run("vector", lambda e: e.tensor_scalar(
                    out=x_bf[:, :], in0=ps[0][:, :], scalar1=bx_sb[:, 0:1],
                    scalar2=None, op0=ALU.add))
                v_B = p.run("scalar", lambda e: e.copy(out=B_bf[:, :],
                                                       in_=ps[1][:, :]))
                v_dtd = p.run("scalar", lambda e: e.copy(out=dtd_bf[:, :],
                                                         in_=ps[2][:, :]))

                v_sp_done = 0
                for ct in range(NCT):
                    for k in range(16):
                        p.run("tensor", lambda e, k=k, ct=ct: e.matmul(
                            ps[3][:, :], wz_bf[:, k * 512 + ct * 128:
                                               k * 512 + (ct + 1) * 128], rhs(k),
                            start=(k == 0), stop=(k == 15)))
                        p.run("tensor", lambda e, k=k, ct=ct: e.matmul(
                            ps[4][:, :], wc_bf[:, k * 512 + ct * 128:
                                               k * 512 + (ct + 1) * 128], rhs(k),
                            start=(k == 0), stop=(k == 15)),
                            inc=(None if k < 15 else "te"))
                    if ct == 0:
                        p.wait("tensor", "sc", v_dtd)
                    p.run("tensor", lambda e, ct=ct: e.matmul(
                        ps[5][:, :], wdt_bf[:, ct * 128:(ct + 1) * 128],
                        dtd_bf[:, :], start=True, stop=True), inc="te")
                    v_zc = p.cnt["te"]
                    p.wait("scalar", "te", v_zc)
                    p.wait("vector", "te", v_zc)
                    v_sz = p.run("scalar", lambda e, ct=ct: e.activation(
                        out=sz_sb[:, ct * TC:(ct + 1) * TC], in_=ps[3][:, :],
                        func=AF.Silu, bias=bz_sb[:, ct:ct + 1]))
                    v_C = p.run("scalar", lambda e, ct=ct: e.copy(
                        out=C_bf[:, ct * TC:(ct + 1) * TC], in_=ps[4][:, :]))
                    v_r = p.run("vector", lambda e, ct=ct: e.tensor_scalar(
                        out=relu_sb[:, :], in0=ps[5][:, :],
                        scalar1=bdt2_sb[:, ct:ct + 1], scalar2=0.0,
                        op0=ALU.add, op1=ALU.max))
                    v_a1 = p.run("scalar", lambda e, ct=ct: e.activation(
                        out=t1_sb[:, :], in_=ps[5][:, :], func=AF.Abs,
                        bias=bdt2_sb[:, ct:ct + 1]))
                    p.run("scalar", lambda e: e.activation(
                        out=t2_sb[:, :], in_=t1_sb[:, :], func=AF.Exp,
                        scale=p.const(-1.0)))
                    v_a3 = p.run("scalar", lambda e: e.activation(
                        out=t1_sb[:, :], in_=t2_sb[:, :], func=AF.Ln,
                        bias=p.const(1.0)))
                    p.wait("vector", "sc", v_a3)
                    v_delta = p.run("vector", lambda e, ct=ct: e.tensor_tensor(
                        out=delta_sb[:, ct * TC:(ct + 1) * TC],
                        in0=relu_sb[:, :], in1=t1_sb[:, :], op=ALU.add))
                    p.wait("tensor", "sc", v_a3)
                    p.wait("tensor", "ve", v_delta)

                p.wait("tensor", "ve", v_x)
                for ct in range(NCT):
                    v_mm = p.run("tensor", lambda e, ct=ct: e.matmul(
                        ps[0][:, :], selx_bf[:, ct * 128:(ct + 1) * 128],
                        x_bf[:, :], start=True, stop=True), inc="te")
                    p.wait("scalar", "te", v_mm)
                    if ct == 0:
                        p.wait("scalar", "ve", v_u_last)
                    v_xr = p.run("scalar", lambda e, ct=ct: e.copy(
                        out=xrep[ct][:, 3:3 + TC], in_=ps[0][:, :]))
                    p.wait("tensor", "sc", v_xr)
                    p.wait("vector", "sc", v_xr)
                    p.run("vector", lambda e, ct=ct: e.tensor_copy(
                        out=xrep[ct][:, 0:3], in_=tail[ct][:, :]))
                    p.run("vector", lambda e, ct=ct: e.tensor_scalar(
                        out=acc_sb[0][:, :], in0=xrep[ct][:, 0:TC],
                        scalar1=convw_sb[:, ct * 4:ct * 4 + 1], scalar2=None,
                        op0=ALU.mult))
                    for k in range(1, 4):
                        p.run("vector", lambda e, ct=ct, k=k: e.scalar_tensor_tensor(
                            out=acc_sb[k % 2][:, :], in0=xrep[ct][:, k:k + TC],
                            scalar=convw_sb[:, ct * 4 + k:ct * 4 + k + 1],
                            in1=acc_sb[(k - 1) % 2][:, :],
                            op0=ALU.mult, op1=ALU.add))
                    v_tl = p.run("vector", lambda e, ct=ct: e.tensor_copy(
                        out=tail[ct][:, :], in_=xrep[ct][:, TC:TC + 3]))
                    p.wait("scalar", "ve", v_tl)
                    v_u_last = p.run("scalar", lambda e, ct=ct: e.activation(
                        out=u_sb[:, ct * TC:(ct + 1) * TC], in_=acc_sb[1][:, :],
                        func=AF.Silu, bias=convb_sb[:, ct:ct + 1]))

                p.wait("vector", "sc", p.cnt["sc"])
                p.wait("tensor", "sc", p.cnt["sc"])
                for ct in range(NCT):
                    p.run("vector", lambda e, ct=ct: e.tensor_tensor(
                        out=du_sb[:, :], in0=delta_sb[:, ct * TC:(ct + 1) * TC],
                        in1=u_sb[:, ct * TC:(ct + 1) * TC], op=ALU.mult))
                    for sc2 in range(SC2):
                        o = sc2 * TSC
                        p.wait("scalar", "ve", v_gate_last)
                        for s in range(DS):
                            p.run("scalar", lambda e, ct=ct, s=s, o=o: e.activation(
                                out=a_buf[:, s * TSC:(s + 1) * TSC],
                                in_=delta_sb[:, ct * TC + o:ct * TC + o + TSC],
                                func=AF.Exp,
                                scale=A_sb[:, ct * DS + s:ct * DS + s + 1]))
                            bank = s % 2
                            p.wait("tensor", "ve", psread[bank])
                            v_te = p.run("tensor", lambda e, ct=ct, s=s, bank=bank,
                                         o=o: e.matmul(
                                ps[bank][:, 0:TSC],
                                selB_bf[:, (ct * DS + s) * 128:
                                        (ct * DS + s + 1) * 128],
                                B_bf[:, o:o + TSC], start=True, stop=True),
                                inc="te")
                            p.wait("vector", "te", v_te)
                            psread[bank] = p.run(
                                "vector", lambda e, s=s, bank=bank, o=o:
                                e.tensor_tensor(
                                    out=b_buf[:, s * TSC:(s + 1) * TSC],
                                    in0=du_sb[:, o:o + TSC],
                                    in1=ps[bank][:, 0:TSC], op=ALU.mult))
                        p.wait("vector", "sc", p.cnt["sc"])
                        for s in range(DS):
                            p.run("vector", lambda e, ct=ct, s=s: e.tensor_tensor_scan(
                                out=h_buf[:, s * TSC:(s + 1) * TSC],
                                data0=a_buf[:, s * TSC:(s + 1) * TSC],
                                data1=b_buf[:, s * TSC:(s + 1) * TSC],
                                initial=hstate[:, ct * DS + s:ct * DS + s + 1],
                                op0=ALU.mult, op1=ALU.add))
                            p.run("vector", lambda e, ct=ct, s=s: e.tensor_copy(
                                out=hstate[:, ct * DS + s:ct * DS + s + 1],
                                in_=h_buf[:, (s + 1) * TSC - 1:(s + 1) * TSC]))
                        for s in range(DS):
                            bank = s % 2
                            p.wait("tensor", "ve", psread[bank])
                            v_te = p.run("tensor", lambda e, ct=ct, s=s, bank=bank,
                                         o=o: e.matmul(
                                ps[bank][:, 0:TSC],
                                selC_bf[:, s * 128:(s + 1) * 128],
                                C_bf[:, ct * TC + o:ct * TC + o + TSC],
                                start=True, stop=True), inc="te")
                            p.wait("vector", "te", v_te)
                            psread[bank] = p.run(
                                "vector", lambda e, s=s, bank=bank:
                                e.tensor_tensor(
                                    out=a_buf[:, s * TSC:(s + 1) * TSC],
                                    in0=h_buf[:, s * TSC:(s + 1) * TSC],
                                    in1=ps[bank][:, 0:TSC], op=ALU.mult))
                        p.run("vector", lambda e: e.tensor_reduce(
                            out=y_sb[:, :],
                            in_=a_buf[:, :].rearrange("p (s t) -> p t s", s=DS),
                            axis=AXL.X, op=ALU.add))
                        p.run("vector", lambda e, ct=ct, o=o: e.scalar_tensor_tensor(
                            out=y_sb[:, :],
                            in0=u_sb[:, ct * TC + o:ct * TC + o + TSC],
                            scalar=D_sb[:, ct:ct + 1], in1=y_sb[:, :],
                            op0=ALU.mult, op1=ALU.add))
                        jbuf = (cc * NCT + ct) % 2
                        p.wait("vector", "d:yg" + str(jbuf), yg_dma[jbuf])
                        v_gate_last = p.run(
                            "vector", lambda e, ct=ct, o=o, jbuf=jbuf:
                            e.tensor_tensor(
                                out=yg_bf[jbuf][:, o:o + TSC], in0=y_sb[:, :],
                                in1=sz_sb[:, ct * TC + o:ct * TC + o + TSC],
                                op=ALU.mult))
                    p.wait("gpsimd", "ve", v_gate_last)
                    yg_dma[jbuf] = p.dma(
                        "gpsimd", yg[ct * 128:(ct + 1) * 128, tok0:tok0 + TC],
                        yg_bf[jbuf][:, :], slot="yg" + str(jbuf))
        p.dwait("gpsimd", "yg0")
        p.dwait("gpsimd", "yg1")
    return p.finalize()


def build_l0_out(tok_per=2048, tc=512):
    NH_T = HID // 128
    ncc = tok_per // tc
    nc = bass.Bass("TRN2", target_bir_lowering=False, debug=False)
    dram = lambda name, shape, dt, kind="ExternalInput": nc.dram_tensor(
        name, shape, dt, kind=kind).ap()
    yg_own = dram("yg_own", [DIN, tok_per], BF16)
    xn0_own = dram("xn0_own", [tok_per, DM], BF16)
    hs_own = dram("hs_own", [tok_per, DM], F32)
    woutT = dram("woutT", [DIN, DM], F32)
    bout_s = dram("bout_s", [128, NDM], F32)
    gT_i32 = dram("gT_i32", [DM, HID], I32)
    uT_i32 = dram("uT_i32", [DM, HID], I32)
    dT_i32 = dram("dT_i32", [HID, DM], I32)
    sg_s = dram("sg_s", [128, NH_T], F32)
    su_s = dram("su_s", [128, NH_T], F32)
    sd_s = dram("sd_s", [128, NDM], F32)
    ident = dram("ident", [128, 128], F32)
    x1f = dram("x1f", [DM, tok_per], F32, "ExternalOutput")
    xn1f = dram("xn1f", [DM, tok_per], BF16, "ExternalOutput")
    wout_bf = dram("wout_bf", [DIN, DM], BF16, "Internal")
    g_bf = dram("g_bf", [DM, HID], BF16, "Internal")
    u_bf = dram("u_bf", [DM, HID], BF16, "Internal")
    d_bf = dram("d_bf", [HID, DM], BF16, "Internal")

    st = ExitStack()
    with st:
        p = P(nc, st, dma_slots=("w0", "w1", "x0", "yg0", "hs0", "o0", "o1a", "o1b",
                                 "cst", "dq0", "dq1"),
              consts=[EPS_LN, -0.5, -128.0])
        sb = lambda name, shape, dt=F32: st.enter_context(
            nc.sbuf_tensor(name, shape, dt))
        pt = lambda name, shape, dt=F32: st.enter_context(
            nc.psum_tensor(name, shape, dt))
        sg_sb = sb("sg_sb", [128, NH_T])
        su_sb = sb("su_sb", [128, NH_T])
        sd_sb = sb("sd_sb", [128, NDM])
        bout_sb = sb("bout_sb", [128, NDM])
        ident_sb = sb("ident_sb", [128, 128])
        sgact = sb("sgact", [128, tc])
        xn1t = [sb(f"xn1t{j}", [128, tc], BF16) for j in range(2)]
        ln = LNBufs2(nc, st, tc, pt)
        ps = [pt(f"ps{i}", [128, 512]) for i in range(4)]

        for ap, sbuf in ((bout_s, bout_sb), (sg_s, sg_sb), (su_s, su_sb),
                         (sd_s, sd_sb), (ident, ident_sb)):
            p.dma("sync", sbuf[:, :], ap, slot="cst")
        p.dwait("scalar", "cst")
        p.dwait("vector", "cst")
        p.dwait("tensor", "cst")
        ln.emit_init(p)

        stA = ExitStack()
        dqs = [stA.enter_context(nc.sbuf_tensor(f"dqs{j}", [128, HID], I32))
               for j in range(2)]
        dqo = stA.enter_context(nc.sbuf_tensor("dqo", [128, HID], BF16))
        wcs = [stA.enter_context(nc.sbuf_tensor(f"wcs{j}", [128, DM], F32))
               for j in range(2)]
        dq_v = [0, 0]
        def dequant(src, dst, nslab, width):
            for k in range(nslab):
                j = k % 2
                p.wait("sync", "ve", dq_v[j])
                p.dma("sync", dqs[j][:, 0:width], src[k * 128:(k + 1) * 128, :],
                      slot=f"dq{j}")
                p.dwait("vector", f"dq{j}")
                p.dwait("vector", "o0")
                v = p.run("vector", lambda e, j=j, w=width: e.tensor_scalar(
                    out=dqo[:, 0:w], in0=dqs[j][:, 0:w], scalar1=-128.0,
                    scalar2=None, op0=ALU.add))
                dq_v[j] = v
                p.wait("gpsimd", "ve", v)
                p.dma("gpsimd", dst[k * 128:(k + 1) * 128, :], dqo[:, 0:width],
                      slot="o0")
        def cast32(src, dst, nslab, width):
            for k in range(nslab):
                j = k % 2
                p.wait("sync", "ve", dq_v[j])
                p.dma("sync", wcs[j][:, 0:width], src[k * 128:(k + 1) * 128, :],
                      slot=f"dq{j}")
                p.dwait("vector", f"dq{j}")
                p.dwait("vector", "o0")
                v = p.run("vector", lambda e, j=j, w=width: e.tensor_copy(
                    out=dqo[:, 0:w], in_=wcs[j][:, 0:w]))
                dq_v[j] = v
                p.wait("gpsimd", "ve", v)
                p.dma("gpsimd", dst[k * 128:(k + 1) * 128, :], dqo[:, 0:width],
                      slot="o0")
        dequant(gT_i32, g_bf, 16, HID)
        dequant(uT_i32, u_bf, 16, HID)
        dequant(dT_i32, d_bf, NH_T, DM)
        cast32(woutT, wout_bf, 32, DM)
        for eng in ("sync", "gpsimd", "vector", "scalar", "tensor"):
            p.dwait(eng, "o0")
        stA.close()
        xnf = sb("xnf", [128, NDM * tc], BF16)
        hst = sb("hst", [128, DM])
        hsf = sb("hsf", [128, NDM * tc])
        x1_sb = sb("x1_sb", [128, NDM * tc])
        ygf = sb("ygf", [128, 32 * tc], BF16)
        gup = sb("gup", [128, NH_T * (tc // 2)], BF16)
        wslab = sb("wslab", [128, 32 * 128], BF16)
        gcol = sb("gcol", [128, 16 * 128], BF16)
        ucol = sb("ucol", [128, 16 * 128], BF16)
        dcol = sb("dcol", [128, NH_T * 128], BF16)

        wout_rd = 0
        ygf_rd = 0
        x1_dma = 0
        xn1_dma = [0, 0]
        for cc in range(ncc):
            t0 = cc * tc
            p.wait_cur("sync", "te")
            p.wait("vector", "d:o0", p.cnt["d:o0"])
            for k in range(32):
                p.dma("sync", ygf[:, k * tc:(k + 1) * tc],
                      yg_own[k * 128:(k + 1) * 128, t0:t0 + tc], slot="yg0")
            for k in range(NDM):
                p.dma("sync", xnf[:, k * tc:(k + 1) * tc],
                      xn0_own[t0:t0 + tc, k * 128:(k + 1) * 128],
                      slot="x0", transpose=True)
            p.wait_cur("tensor", "ve")
            p.wait_cur("tensor", "sc")
            for tt in range(tc // 128):
                p.wait_cur("sync", "te")
                p.dma("sync", hst[:, :], hs_own[t0 + tt * 128:t0 + (tt + 1) * 128, :],
                      slot="hs0")
                p.dwait("tensor", "hs0")
                for i in range(NDM):
                    v_te = p.run("tensor", lambda e, i=i: e.transpose(
                        ps[3][:, 0:128], hst[:, i * 128:(i + 1) * 128],
                        ident_sb[:, :]), inc="te")
                    p.wait("vector", "te", v_te)
                    v_ev = p.run("vector", lambda e, i=i, tt=tt: e.tensor_copy(
                        out=hsf[:, i * tc + tt * 128:i * tc + (tt + 1) * 128],
                        in_=ps[3][:, 0:128]))
                    p.wait("tensor", "ve", v_ev)
            p.dwait("tensor", "yg0")
            p.dwait("tensor", "x0")

            for i in range(NDM):
                p.wait_cur("sync", "te")
                for q in range(8):
                    p.dma("sync", wslab[:, q * 4 * 128:(q + 1) * 4 * 128]
                          .rearrange("p (a f) -> p a f", f=128),
                          wout_bf[q * 512:(q + 1) * 512,
                                  i * 128:(i + 1) * 128].rearrange(
                              "(a p) f -> p a f", p=128),
                          slot="w0")
                p.dwait("tensor", "w0")
                for k in range(32):
                    p.run("tensor", lambda e, i=i, k=k: e.matmul(
                        ps[0][:, 0:tc], wslab[:, k * 128:(k + 1) * 128],
                        ygf[:, k * tc:(k + 1) * tc],
                        start=(k == 0), stop=(k == 31)),
                        inc=("te" if k == 31 else None))
                wout_rd = p.cnt["te"]
                p.wait("vector", "te", wout_rd)
                p.run("vector", lambda e, i=i: e.scalar_tensor_tensor(
                    out=x1_sb[:, i * tc:(i + 1) * tc], in0=ps[0][:, 0:tc],
                    scalar=bout_sb[:, i:i + 1], in1=hsf[:, i * tc:(i + 1) * tc],
                    op0=ALU.add, op1=ALU.add))
                p.wait("tensor", "ve", p.cnt["ve"])

            th = tc // 2
            for half in range(2):
                t1 = half * th
                for h in range(NH_T):
                    p.wait_cur("sync", "te")
                    for q in range(4):
                        p.dma("sync", gcol[:, q * 4 * 128:(q + 1) * 4 * 128].rearrange(
                                  "p (a f) -> p a f", f=128),
                              g_bf[q * 512:(q + 1) * 512,
                                   h * 128:(h + 1) * 128].rearrange(
                                  "(a p) f -> p a f", p=128), slot="w0")
                        p.dma("sync", ucol[:, q * 4 * 128:(q + 1) * 4 * 128].rearrange(
                                  "p (a f) -> p a f", f=128),
                              u_bf[q * 512:(q + 1) * 512,
                                   h * 128:(h + 1) * 128].rearrange(
                                  "(a p) f -> p a f", p=128), slot="w1")
                    p.dwait("tensor", "w0")
                    p.dwait("tensor", "w1")
                    for k in range(16):
                        p.run("tensor", lambda e, k=k, t1=t1: e.matmul(
                            ps[1][:, 0:th], gcol[:, k * 128:(k + 1) * 128],
                            xnf[:, k * tc + t1:k * tc + t1 + th],
                            start=(k == 0), stop=(k == 15)))
                        p.run("tensor", lambda e, k=k, t1=t1: e.matmul(
                            ps[2][:, 0:th], ucol[:, k * 128:(k + 1) * 128],
                            xnf[:, k * tc + t1:k * tc + t1 + th],
                            start=(k == 0), stop=(k == 15)),
                            inc=("te" if k == 15 else None))
                    v_gu = p.cnt["te"]
                    p.wait("scalar", "te", v_gu)
                    v_sg = p.run("scalar", lambda e, h=h: e.activation(
                        out=sgact[:, 0:th], in_=ps[1][:, 0:th], func=AF.Silu,
                        scale=sg_sb[:, h:h + 1]))
                    p.wait("vector", "sc", v_sg)
                    v_gup = p.run("vector", lambda e, h=h: e.scalar_tensor_tensor(
                        out=gup[:, h * th:(h + 1) * th], in0=ps[2][:, 0:th],
                        scalar=su_sb[:, h:h + 1], in1=sgact[:, 0:th],
                        op0=ALU.mult, op1=ALU.mult))
                    p.wait("tensor", "ve", v_gup)
                    p.wait("tensor", "sc", v_sg)

                for i in range(NDM):
                    p.wait_cur("sync", "te")
                    for q in range(11):
                        p.dma("sync", dcol[:, q * 4 * 128:(q + 1) * 4 * 128].rearrange(
                                  "p (a f) -> p a f", f=128),
                              d_bf[q * 512:(q + 1) * 512,
                                   i * 128:(i + 1) * 128].rearrange(
                                  "(a p) f -> p a f", p=128), slot="w0")
                    p.dwait("tensor", "w0")
                    for k in range(NH_T):
                        p.run("tensor", lambda e, i=i, k=k: e.matmul(
                            ps[0][:, 0:th], dcol[:, k * 128:(k + 1) * 128],
                            gup[:, k * th:(k + 1) * th],
                            start=(k == 0), stop=(k == NH_T - 1)),
                            inc=("te" if k == NH_T - 1 else None))
                    v_dn = p.cnt["te"]
                    p.wait("vector", "te", v_dn)
                    v_x1 = p.run("vector", lambda e, i=i, t1=t1: e.scalar_tensor_tensor(
                        out=x1_sb[:, i * tc + t1:i * tc + t1 + th],
                        in0=ps[0][:, 0:th], scalar=sd_sb[:, i:i + 1],
                        in1=x1_sb[:, i * tc + t1:i * tc + t1 + th],
                        op0=ALU.mult, op1=ALU.add))
                    p.wait("tensor", "ve", v_x1)
            for i in range(NDM):
                p.wait("gpsimd", "ve", p.cnt["ve"])
                x1_dma = p.dma("gpsimd", x1f[i * 128:(i + 1) * 128, t0:t0 + tc],
                               x1_sb[:, i * tc:(i + 1) * tc], slot="o0")

            def in_tile(i):
                return x1_sb[:, i * tc:(i + 1) * tc]

            def out_tile(i):
                p.dwait("vector", "o1" + "ab"[i % 2])
                return xn1t[i % 2][:, :]

            def out_done(i, ve_val, t0=t0):
                p.wait("gpsimd", "ve", ve_val)
                p.dma("gpsimd", xn1f[i * 128:(i + 1) * 128, t0:t0 + tc],
                      xn1t[i % 2][:, :], slot="o1" + "ab"[i % 2])

            p.wait("scalar", "ve", p.cnt["ve"])
            p.wait("tensor", "ve", p.cnt["ve"])
            emit_ln_chunk(p, ln, NDM, in_tile, out_tile, out_done)
        p.dwait("gpsimd", "o0")
        p.dwait("gpsimd", "o1a")
        p.dwait("gpsimd", "o1b")
    return p.finalize()


class LNBufs2:

    def __init__(self, nc, st, tc, pt):
        sb = lambda name, shape, dt=F32: st.enter_context(
            nc.sbuf_tensor(name, shape, dt))
        self.tc = tc
        self.sq = [sb(f"ln_sq{j}", [128, tc]) for j in range(2)]
        self.scr = [sb(f"ln_scr{j}", [128, tc]) for j in range(2)]
        self.ones_inv = sb("ln_ones_inv", [128, 1])
        self.ones_bc = sb("ln_ones_bc", [1, 128])
        self.m = sb("ln_m", [1, tc])
        self.msq = sb("ln_msq", [1, tc])
        self.var = sb("ln_var", [1, tc])
        self.rstd = sb("ln_rstd", [1, tc])
        self.negm = sb("ln_negm", [1, tc])
        self.ps_s = pt("ln_ps_s", [128, tc])
        self.ps_s2 = pt("ln_ps_s2", [128, tc])
        self.ps_mb = pt("ln_ps_mb", [128, tc])
        self.ps_rb = pt("ln_ps_rb", [128, tc])

    def emit_init(self, p):
        p.run("gpsimd", lambda e: e.memset(self.ones_inv[:, :], 1.0 / DM))
        p.run("gpsimd", lambda e: e.memset(self.ones_bc[0:1, :], 1.0), inc="gp")
        p.wait_cur("tensor", "gp")
        p.wait_cur("scalar", "gp")
        p.wait_cur("vector", "gp")


def build_l1_attn(nbatch=NBATCH, ltok=L):
    ntok = nbatch * ltok
    nkt = ltok // 128
    nqc = ltok // 512
    nc = bass.Bass("TRN2", target_bir_lowering=False, debug=False)
    dram = lambda name, shape, dt, kind="ExternalInput": nc.dram_tensor(
        name, shape, dt, kind=kind).ap()
    xn1 = dram("xn1", [DM, ntok], BF16)
    wqT_s = dram("wqT_s", [DM, 256], F32)
    wkT_s = dram("wkT_s", [DM, 256], F32)
    wvT_s = dram("wvT_s", [DM, 256], F32)
    identb = dram("identb", [128, 128], BF16)
    maskd = dram("maskd", [128, 4 * 512], BF16)
    maskz = dram("maskz", [128, 4 * 512], BF16)
    merged = dram("merged", [256, ntok], BF16, "ExternalOutput")

    st = ExitStack()
    with st:
        p = P(nc, st, dma_slots=("w0", "w1", "x0", "o0", "o1", "cst"),
              consts=[float(ASCALE), -float(ASCALE), 1e-6, 1.0])
        sb = lambda name, shape, dt=F32: st.enter_context(
            nc.sbuf_tensor(name, shape, dt))
        pt = lambda name, shape, dt=F32: st.enter_context(
            nc.psum_tensor(name, shape, dt))
        wq_bf = sb("wq_bf", [128, 16 * 256], BF16)
        wk_bf = sb("wk_bf", [128, 16 * 256], BF16)
        wv_bf = sb("wv_bf", [128, 16 * 256], BF16)
        wstg = [sb(f"wstg{j}", [128, 256], F32) for j in range(2)]
        identb_sb = sb("identb_sb", [128, 128], BF16)
        maskd_sb = sb("maskd_sb", [128, 4 * 512], BF16)
        maskz_sb = sb("maskz_sb", [128, 4 * 512], BF16)
        ones_key = sb("ones_key", [128, 1], BF16)
        ones_f32 = sb("ones_f32", [128, 1], F32)
        ones_bc1 = sb("ones_bc1", [1, 128], F32)
        xnf = sb("xnf", [128, 16 * 512], BF16)
        Qf = [sb(f"Qf{j}", [128, ltok], BF16) for j in range(2)]
        Kf = [sb(f"Kf{j}", [128, ltok], BF16) for j in range(2)]
        Vf = [sb(f"Vf{j}", [128, ltok], BF16) for j in range(2)]
        Vt = sb("Vt", [128, ltok], BF16)
        e_sb = sb("e_sb", [128, 512], BF16)
        ed_sb = sb("ed_sb", [128, 512], BF16)
        ez_sb = sb("ez_sb", [128, 512], BF16)
        o_sb = sb("o_sb", [128, 512])
        sqs = sb("sqs", [128, 512])
        nrm = sb("nrm", [1, 8])
        bnd_sb = sb("bnd_sb", [128, 1])
        d_sb = sb("d_sb", [1, 512])
        den_sb = sb("den_sb", [1, 512])
        mrg = [sb(f"mrg{j}", [128, 512], BF16) for j in range(2)]
        ps_s = [pt(f"ps_s{j}", [128, 512]) for j in range(2)]
        ps_o = pt("ps_o", [128, 512])
        ps_z = pt("ps_z", [128, 512])
        ps_d = pt("ps_d", [128, 512])
        ps_b = pt("ps_b", [128, 512])
        ps_t = pt("ps_t", [128, 512], BF16)

        castv = [0, 0]
        for w_dram, w_sb in ((wqT_s, wq_bf), (wkT_s, wk_bf), (wvT_s, wv_bf)):
            for k in range(16):
                j = k % 2
                p.wait("sync", "ve", castv[j])
                p.dma("sync", wstg[j][:, :], w_dram[k * 128:(k + 1) * 128, :],
                      slot=f"w{j}")
                p.dwait("vector", f"w{j}")
                castv[j] = p.run("vector", lambda e, k=k, j=j, w_sb=w_sb:
                                 e.tensor_copy(out=w_sb[:, k * 256:(k + 1) * 256],
                                               in_=wstg[j][:, :]))
        for ap, sbuf in ((identb, identb_sb), (maskd, maskd_sb),
                         (maskz, maskz_sb)):
            p.dma("sync", sbuf[:, :], ap, slot="cst")
        p.dwait("tensor", "cst")
        p.dwait("vector", "cst")
        p.run("gpsimd", lambda e: e.memset(ones_key[:, :], 1.0))
        p.run("gpsimd", lambda e: e.memset(ones_f32[:, :], 1.0))
        vg = p.run("gpsimd", lambda e: e.memset(ones_bc1[0:1, :], 1.0), inc="gp")
        p.wait("tensor", "gp", p.cnt["gp"])
        p.wait_cur("tensor", "ve")

        mrg_dma = [0, 0]
        for b in range(nbatch):
            for cc in range(ltok // 512):
                tok0 = b * ltok + cc * 512
                p.wait_cur("sync", "te")
                for k in range(16):
                    p.dma("sync", xnf[:, k * 512:(k + 1) * 512],
                          xn1[k * 128:(k + 1) * 128, tok0:tok0 + 512], slot="x0")
                p.dwait("tensor", "x0")
                mats = ((wq_bf, Qf), (wk_bf, Kf), (wv_bf, Vf))
                for k in range(16):
                    for mi, (w_sb, _) in enumerate(mats):
                        for hl in range(2):
                            bank = mi * 2 + hl
                            tgt = (ps_s + [ps_o, ps_z, ps_d, ps_b])[bank]
                            p.run("tensor", lambda e, k=k, w_sb=w_sb, hl=hl,
                                  tgt=tgt: e.matmul(
                                      tgt[:, :],
                                      w_sb[:, k * 256 + hl * 128:
                                           k * 256 + (hl + 1) * 128],
                                      xnf[:, k * 512:(k + 1) * 512],
                                      start=(k == 0), stop=(k == 15)),
                                  inc=("te" if k == 15 and bank == 5 else None))
                v_qkv = p.cnt["te"]
                p.wait("scalar", "te", v_qkv)
                for mi, (_, dst) in enumerate(mats):
                    for hl in range(2):
                        bank = mi * 2 + hl
                        tgt = (ps_s + [ps_o, ps_z, ps_d, ps_b])[bank]
                        p.run("scalar", lambda e, dst=dst, hl=hl, cc=cc, tgt=tgt:
                              e.copy(out=dst[hl][:, cc * 512:(cc + 1) * 512],
                                     in_=tgt[:, :]))
                v_ev = p.cnt["sc"]
                p.wait("tensor", "sc", v_ev)

            for hl in range(2):
                p.wait_cur("tensor", "sc")
                p.wait_cur("tensor", "ve")
                for kt in range(nkt):
                    v_t = p.run("tensor", lambda e, kt=kt, hl=hl: e.transpose(
                        ps_t[:, 0:128], Vf[hl][:, kt * 128:(kt + 1) * 128],
                        identb_sb[:, :]), inc="te")
                    p.wait("vector", "te", v_t)
                    v_ev = p.run("vector", lambda e, kt=kt: e.tensor_copy(
                        out=Vt[:, kt * 128:(kt + 1) * 128], in_=ps_t[:, 0:128]))
                    p.wait("tensor", "ve", v_ev)
                vz = p.run("vector", lambda e: e.memset(nrm[0:1, :], 0.0))
                p.wait("scalar", "ve", vz)
                for src_f, col in ((Qf[hl], 0), (Kf[hl], 1)):
                    for cc in range(ltok // 512):
                        v_sq = p.run("scalar", lambda e, src_f=src_f, cc=cc:
                                     e.activation(out=sqs[:, :],
                                                  in_=src_f[:, cc * 512:
                                                            (cc + 1) * 512],
                                                  func=AF.Square))
                        p.wait("tensor", "sc", v_sq)
                        v_mm = p.run("tensor", lambda e: e.matmul(
                            ps_b[0:1, :], ones_f32[:, 0:1], sqs[:, :],
                            start=True, stop=True), inc="te")
                        p.wait("vector", "te", v_mm)
                        p.run("vector", lambda e: e.tensor_reduce(
                            out=nrm[0:1, 7:8], in_=ps_b[0:1, :], axis=AXL.X,
                            op=ALU.max))
                        v_mx = p.run("vector", lambda e, col=col: e.tensor_tensor(
                            out=nrm[0:1, col:col + 1], in0=nrm[0:1, col:col + 1],
                            in1=nrm[0:1, 7:8], op=ALU.max))
                        p.wait("tensor", "ve", v_mx)
                        p.wait("scalar", "ve", v_mx)
                p.run("vector", lambda e: e.tensor_tensor(
                    out=nrm[0:1, 2:3], in0=nrm[0:1, 0:1], in1=nrm[0:1, 1:2],
                    op=ALU.mult), inc="ve")
                p.wait_cur("scalar", "ve")
                p.run("scalar", lambda e: e.activation(
                    out=nrm[0:1, 3:4], in_=nrm[0:1, 2:3], func=AF.Sqrt))
                v_b = p.run("scalar", lambda e: e.activation(
                    out=nrm[0:1, 4:5], in_=nrm[0:1, 3:4], func=AF.Copy,
                    scale=p.const(-float(ASCALE))[0:1, :]))
                p.wait("tensor", "sc", v_b)
                v_bb = p.run("tensor", lambda e: e.matmul(
                    ps_b[:, 0:1], ones_bc1[0:1, :], nrm[0:1, 4:5],
                    start=True, stop=True), inc="te")
                p.wait("vector", "te", v_bb)
                v_bc = p.run("vector", lambda e: e.tensor_copy(
                    out=bnd_sb[:, :], in_=ps_b[:, 0:1]))
                p.wait("scalar", "ve", v_bc)
                p.wait("tensor", "ve", v_bc)

                for qc in range(nqc):
                    last_d_kt = 4 * qc + 3
                    for kt in range(nkt):
                        bank = kt % 2
                        v_s = p.run("tensor", lambda e, kt=kt, qc=qc, hl=hl,
                                    bank=bank: e.matmul(
                                        ps_s[bank][:, :],
                                        Kf[hl][:, kt * 128:(kt + 1) * 128],
                                        Qf[hl][:, qc * 512:(qc + 1) * 512],
                                        start=True, stop=True), inc="te")
                        p.wait("scalar", "te", v_s)
                        v_e = p.run("scalar", lambda e, bank=bank: e.activation(
                            out=e_sb[:, :], in_=ps_s[bank][:, :], func=AF.Exp,
                            scale=p.const(float(ASCALE)), bias=bnd_sb[:, 0:1]))
                        use_d = kt <= last_d_kt
                        diag = 4 * qc <= kt <= last_d_kt
                        if diag:
                            jj = kt - 4 * qc
                            p.wait("vector", "sc", v_e)
                            v_ed = p.run("vector", lambda e, jj=jj: e.tensor_tensor(
                                out=ed_sb[:, :], in0=e_sb[:, :],
                                in1=maskd_sb[:, jj * 512:(jj + 1) * 512],
                                op=ALU.mult))
                            v_ez = p.run("vector", lambda e, jj=jj: e.tensor_tensor(
                                out=ez_sb[:, :], in0=e_sb[:, :],
                                in1=maskz_sb[:, jj * 512:(jj + 1) * 512],
                                op=ALU.mult))
                            p.wait("tensor", "ve", v_ez)
                            e_for_z, e_for_d = ez_sb, ed_sb
                        else:
                            p.wait("tensor", "sc", v_e)
                            e_for_z, e_for_d = e_sb, e_sb
                        if use_d:
                            p.run("tensor", lambda e, kt=kt, e_for_d=e_for_d,
                                  ld=last_d_kt:
                                  e.matmul(ps_d[0:1, :], ones_key[:, 0:1],
                                           e_for_d[:, :], start=(kt == 0),
                                           stop=(kt == ld)))
                            p.run("tensor", lambda e, kt=kt, e_for_d=e_for_d,
                                  ld=last_d_kt:
                                  e.matmul(ps_o[:, :],
                                           Vt[:, kt * 128:(kt + 1) * 128],
                                           e_for_d[:, :], start=(kt == 0),
                                           stop=(kt == ld)))
                        p.run("tensor", lambda e, kt=kt, e_for_z=e_for_z:
                              e.matmul(ps_z[0:1, :], ones_key[:, 0:1],
                                       e_for_z[:, :], start=(kt == 0),
                                       stop=(kt == nkt - 1)), inc="te")
                        p.wait("scalar", "te", p.cnt["te"])
                        p.wait("vector", "te", p.cnt["te"])
                    v_all = p.cnt["te"]
                    p.wait("vector", "te", v_all)
                    p.run("vector", lambda e: e.tensor_copy(out=d_sb[0:1, :],
                                                            in_=ps_d[0:1, :]))
                    p.run("vector", lambda e: e.scalar_tensor_tensor(
                        out=den_sb[0:1, :], in0=ps_z[0:1, :],
                        scalar=p.const(1e-6)[0:1, :], in1=d_sb[0:1, :],
                        op0=ALU.mult, op1=ALU.add))
                    v_den = p.run("vector", lambda e: e.reciprocal(
                        out=den_sb[0:1, :], in_=den_sb[0:1, :]))
                    p.wait("tensor", "ve", v_den)
                    v_bc2 = p.run("tensor", lambda e: e.matmul(
                        ps_b[:, :], ones_bc1[0:1, :], den_sb[0:1, :],
                        start=True, stop=True), inc="te")
                    p.wait("scalar", "te", v_bc2)
                    v_oe = p.run("scalar", lambda e: e.copy(out=o_sb[:, :],
                                                            in_=ps_o[:, :]))
                    jb = qc % 2
                    p.dwait("vector", f"o{jb}")
                    p.wait("vector", "sc", v_oe)
                    v_m = p.run("vector", lambda e, jb=jb: e.tensor_tensor(
                        out=mrg[jb][:, :], in0=o_sb[:, :], in1=ps_b[:, :],
                        op=ALU.mult))
                    p.wait("gpsimd", "ve", v_m)
                    p.dma("gpsimd",
                          merged[hl * 128:(hl + 1) * 128,
                                 b * ltok + qc * 512:b * ltok + (qc + 1) * 512],
                          mrg[jb][:, :], slot=f"o{jb}")
                    p.wait("tensor", "ve", p.cnt["ve"])
                    p.wait("tensor", "sc", p.cnt["sc"])
        p.dwait("gpsimd", "o0")
        p.dwait("gpsimd", "o1")
    return p.finalize()


def build_l1_out(tok_per=2048, tc=512):
    NH_T = HID // 128
    ncc = tok_per // tc
    nc = bass.Bass("TRN2", target_bir_lowering=False, debug=False)
    dram = lambda name, shape, dt, kind="ExternalInput": nc.dram_tensor(
        name, shape, dt, kind=kind).ap()
    mrg_own = dram("mrg_own", [DM, tok_per], BF16)
    xn1_own = dram("xn1_own", [DM, tok_per], BF16)
    x1_own = dram("x1_own", [DM, tok_per], F32)
    woT = dram("woT", [DM, DM], F32)
    w1T = dram("w1T", [DM, HID], F32)
    w2T = dram("w2T", [DM, HID], F32)
    w3T = dram("w3T", [HID, DM], F32)
    ident = dram("ident", [128, 128], F32)
    final = dram("final", [tok_per, DM], F32, "ExternalOutput")
    wo_bf = dram("wo_bf", [DM, DM], BF16, "Internal")
    w1q = dram("w1q", [DM, HID], BF16, "Internal")
    w2q = dram("w2q", [DM, HID], BF16, "Internal")
    w3q = dram("w3q", [HID, DM], BF16, "Internal")

    st = ExitStack()
    with st:
        p = P(nc, st, dma_slots=("w0", "w1", "x0", "o0", "cst", "dq0", "dq1"),
              consts=[0.5, -0.5, 1e-8])
        sb = lambda name, shape, dt=F32: st.enter_context(
            nc.sbuf_tensor(name, shape, dt))
        pt = lambda name, shape, dt=F32: st.enter_context(
            nc.psum_tensor(name, shape, dt))
        ident_sb = sb("ident_sb", [128, 128])
        ones_f32 = sb("ones_f32", [128, 1])
        ones_bc1 = sb("ones_bc1", [1, 128])
        gacc = sb("gacc", [128, 1])
        gsc = sb("gsc", [1, 4])
        recip_bc = sb("recip_bc", [128, 1])
        ps = [pt(f"ps{i}", [128, 512]) for i in range(6)]

        p.dma("sync", ident_sb[:, :], ident, slot="cst")
        p.dwait("tensor", "cst")
        p.run("gpsimd", lambda e: e.memset(ones_f32[:, :], 1.0))
        p.run("gpsimd", lambda e: e.memset(ones_bc1[0:1, :], 1.0), inc="gp")
        p.wait_cur("tensor", "gp")
        p.wait_cur("vector", "gp")
        p.wait_cur("scalar", "gp")

        stA = ExitStack()
        qs = [stA.enter_context(nc.sbuf_tensor(f"qs{j}", [128, HID], F32))
              for j in range(2)]
        qt = stA.enter_context(nc.sbuf_tensor("qt", [128, HID], F32))
        qm1 = stA.enter_context(nc.sbuf_tensor("qm1", [128, HID], F32))
        qo = stA.enter_context(nc.sbuf_tensor("qo", [128, HID], BF16))
        rd_v = [0, 0]

        def quantize(w_dram, q_dram, nslab, width, numel):
            vz = p.run("vector", lambda e: e.memset(gacc[:, :], 0.0))
            for k in range(nslab):
                j = k % 2
                p.wait("sync", "ve", rd_v[j])
                p.dma("sync", qs[j][:, 0:width], w_dram[k * 128:(k + 1) * 128, :],
                      slot=f"dq{j}")
                p.dwait("vector", f"dq{j}")
                p.run("vector", lambda e, j=j, w=width: e.tensor_reduce(
                    out=gacc[:, 0:1] if False else qt[:, 0:1], in_=qs[j][:, 0:w],
                    axis=AXL.X, op=ALU.add, apply_absolute_value=True))
                rd_v[j] = p.run("vector", lambda e: e.tensor_tensor(
                    out=gacc[:, :], in0=gacc[:, :], in1=qt[:, 0:1], op=ALU.add))
            v_g = p.cnt["ve"]
            p.wait("tensor", "ve", v_g)
            v_mm = p.run("tensor", lambda e: e.matmul(
                ps[0][0:1, 0:1], ones_f32[:, 0:1], gacc[:, 0:1],
                start=True, stop=True), inc="te")
            p.wait("vector", "te", v_mm)
            p.run("vector", lambda e, n=numel: e.scalar_tensor_tensor(
                out=gsc[0:1, 0:1], in0=ps[0][0:1, 0:1], scalar=1.0 / n,
                in1=p.const(1e-8)[0:1, :], op0=ALU.mult, op1=ALU.add))
            v_r = p.run("vector", lambda e: e.reciprocal(
                out=gsc[0:1, 1:2], in_=gsc[0:1, 0:1]))
            p.wait("tensor", "ve", v_r)
            v_bb = p.run("tensor", lambda e: e.matmul(
                ps[0][:, 0:1], ones_bc1[0:1, :], gsc[0:1, 1:2],
                start=True, stop=True), inc="te")
            p.wait("vector", "te", v_bb)
            v_rb = p.run("vector", lambda e: e.tensor_copy(out=recip_bc[:, :],
                                                           in_=ps[0][:, 0:1]))
            p.wait("tensor", "ve", v_rb)
            for k in range(nslab):
                j = k % 2
                p.wait("sync", "ve", rd_v[j])
                p.dma("sync", qs[j][:, 0:width], w_dram[k * 128:(k + 1) * 128, :],
                      slot=f"dq{j}")
                p.dwait("vector", f"dq{j}")
                p.run("vector", lambda e, j=j, w=width: e.tensor_scalar(
                    out=qt[:, 0:w], in0=qs[j][:, 0:w], scalar1=recip_bc[:, 0:1],
                    scalar2=None, op0=ALU.mult))
                rd_v[j] = p.cnt["ve"]
                p.run("vector", lambda e, w=width: e.tensor_scalar(
                    out=qm1[:, 0:w], in0=qt[:, 0:w], scalar1=0.5, scalar2=None,
                    op0=ALU.is_gt))
                p.dwait("vector", "o0")
                p.run("vector", lambda e, w=width: e.tensor_scalar(
                    out=qt[:, 0:w], in0=qt[:, 0:w], scalar1=-0.5, scalar2=None,
                    op0=ALU.is_lt))
                v_q = p.run("vector", lambda e, w=width: e.tensor_tensor(
                    out=qo[:, 0:w], in0=qm1[:, 0:w], in1=qt[:, 0:w],
                    op=ALU.subtract))
                p.wait("gpsimd", "ve", v_q)
                p.dma("gpsimd", q_dram[k * 128:(k + 1) * 128, :], qo[:, 0:width],
                      slot="o0")

        def cast_wo():
            for k in range(16):
                j = k % 2
                p.wait("sync", "ve", rd_v[j])
                p.dma("sync", qs[j][:, 0:DM], woT[k * 128:(k + 1) * 128, :],
                      slot=f"dq{j}")
                p.dwait("vector", f"dq{j}")
                p.dwait("vector", "o0")
                rd_v[j] = p.run("vector", lambda e, j=j: e.tensor_copy(
                    out=qo[:, 0:DM], in_=qs[j][:, 0:DM]))
                p.wait("gpsimd", "ve", rd_v[j])
                p.dma("gpsimd", wo_bf[k * 128:(k + 1) * 128, :], qo[:, 0:DM],
                      slot="o0")

        quantize(w1T, w1q, 16, HID, DM * HID)
        quantize(w2T, w2q, 16, HID, DM * HID)
        quantize(w3T, w3q, NH_T, DM, DM * HID)
        cast_wo()
        for eng in ("sync", "gpsimd", "vector", "scalar", "tensor"):
            p.dwait(eng, "o0")
        stA.close()

        mrgf = sb("mrgf", [128, NDM * tc], BF16)
        xnf = sb("xnf", [128, NDM * tc], BF16)
        x1c = sb("x1c", [128, NDM * tc])
        hid = sb("hid", [128, NH_T * tc], BF16)
        wcol = sb("wcol", [128, 16 * 128], BF16)
        w1col = sb("w1col", [128, 16 * 128], BF16)
        w2col = sb("w2col", [128, 16 * 128], BF16)
        w3col = sb("w3col", [128, NH_T * 128], BF16)
        sgact = sb("sgact", [128, tc])
        ot_sb = sb("ot_sb", [128, DM])

        for cc in range(ncc):
            t0 = cc * tc
            p.wait_cur("sync", "te")
            p.wait_cur("sync", "ve")
            for k in range(NDM):
                p.dma("sync", mrgf[:, k * tc:(k + 1) * tc],
                      mrg_own[k * 128:(k + 1) * 128, t0:t0 + tc], slot="x0")
                p.dma("sync", xnf[:, k * tc:(k + 1) * tc],
                      xn1_own[k * 128:(k + 1) * 128, t0:t0 + tc], slot="x0")
                p.dma("sync", x1c[:, k * tc:(k + 1) * tc],
                      x1_own[k * 128:(k + 1) * 128, t0:t0 + tc], slot="x0")
            p.dwait("tensor", "x0")
            p.dwait("vector", "x0")

            for i in range(NDM):
                p.wait_cur("sync", "te")
                for q in range(4):
                    p.dma("sync", wcol[:, q * 4 * 128:(q + 1) * 4 * 128]
                          .rearrange("p (a f) -> p a f", f=128),
                          wo_bf[q * 512:(q + 1) * 512,
                                i * 128:(i + 1) * 128].rearrange(
                              "(a p) f -> p a f", p=128), slot="w0")
                p.dwait("tensor", "w0")
                for k in range(16):
                    p.run("tensor", lambda e, k=k: e.matmul(
                        ps[0][:, 0:tc], wcol[:, k * 128:(k + 1) * 128],
                        mrgf[:, k * tc:(k + 1) * tc],
                        start=(k == 0), stop=(k == 15)),
                        inc=("te" if k == 15 else None))
                p.wait("vector", "te", p.cnt["te"])
                v_a = p.run("vector", lambda e, i=i: e.tensor_tensor(
                    out=x1c[:, i * tc:(i + 1) * tc], in0=ps[0][:, 0:tc],
                    in1=x1c[:, i * tc:(i + 1) * tc], op=ALU.add))
                p.wait("tensor", "ve", v_a)

            for h in range(NH_T):
                p.wait_cur("sync", "te")
                for q in range(4):
                    p.dma("sync", w1col[:, q * 4 * 128:(q + 1) * 4 * 128]
                          .rearrange("p (a f) -> p a f", f=128),
                          w1q[q * 512:(q + 1) * 512,
                              h * 128:(h + 1) * 128].rearrange(
                              "(a p) f -> p a f", p=128), slot="w0")
                    p.dma("sync", w2col[:, q * 4 * 128:(q + 1) * 4 * 128]
                          .rearrange("p (a f) -> p a f", f=128),
                          w2q[q * 512:(q + 1) * 512,
                              h * 128:(h + 1) * 128].rearrange(
                              "(a p) f -> p a f", p=128), slot="w1")
                p.dwait("tensor", "w0")
                p.dwait("tensor", "w1")
                for k in range(16):
                    p.run("tensor", lambda e, k=k: e.matmul(
                        ps[1][:, 0:tc], w1col[:, k * 128:(k + 1) * 128],
                        xnf[:, k * tc:(k + 1) * tc],
                        start=(k == 0), stop=(k == 15)))
                    p.run("tensor", lambda e, k=k: e.matmul(
                        ps[2][:, 0:tc], w2col[:, k * 128:(k + 1) * 128],
                        xnf[:, k * tc:(k + 1) * tc],
                        start=(k == 0), stop=(k == 15)),
                        inc=("te" if k == 15 else None))
                v_12 = p.cnt["te"]
                p.wait("scalar", "te", v_12)
                v_sg = p.run("scalar", lambda e: e.activation(
                    out=sgact[:, :], in_=ps[1][:, 0:tc], func=AF.Sigmoid))
                p.wait("vector", "sc", v_sg)
                v_h = p.run("vector", lambda e, h=h: e.tensor_tensor(
                    out=hid[:, h * tc:(h + 1) * tc], in0=sgact[:, :],
                    in1=ps[2][:, 0:tc], op=ALU.mult))
                p.wait("tensor", "ve", v_h)
                p.wait("tensor", "sc", v_sg)

            for i in range(NDM):
                p.wait_cur("sync", "te")
                for q in range(11):
                    p.dma("sync", w3col[:, q * 4 * 128:(q + 1) * 4 * 128]
                          .rearrange("p (a f) -> p a f", f=128),
                          w3q[q * 512:(q + 1) * 512,
                              i * 128:(i + 1) * 128].rearrange(
                              "(a p) f -> p a f", p=128), slot="w0")
                p.dwait("tensor", "w0")
                for k in range(NH_T):
                    p.run("tensor", lambda e, i=i, k=k: e.matmul(
                        ps[0][:, 0:tc], w3col[:, k * 128:(k + 1) * 128],
                        hid[:, k * tc:(k + 1) * tc],
                        start=(k == 0), stop=(k == NH_T - 1)),
                        inc=("te" if k == NH_T - 1 else None))
                p.wait("vector", "te", p.cnt["te"])
                v_f = p.run("vector", lambda e, i=i: e.tensor_tensor(
                    out=x1c[:, i * tc:(i + 1) * tc], in0=ps[0][:, 0:tc],
                    in1=x1c[:, i * tc:(i + 1) * tc], op=ALU.add))
                p.wait("tensor", "ve", v_f)

            for tt in range(tc // 128):
                for i in range(NDM):
                    v_t = p.run("tensor", lambda e, i=i, tt=tt: e.transpose(
                        ps[3][:, 0:128],
                        x1c[:, i * tc + tt * 128:i * tc + (tt + 1) * 128],
                        ident_sb[:, :]), inc="te")
                    p.wait("vector", "te", v_t)
                    v_ev = p.run("vector", lambda e, i=i: e.tensor_copy(
                        out=ot_sb[:, i * 128:(i + 1) * 128], in_=ps[3][:, 0:128]))
                    p.wait("tensor", "ve", v_ev)
                p.wait("gpsimd", "ve", p.cnt["ve"])
                p.dma("gpsimd", final[t0 + tt * 128:t0 + (tt + 1) * 128, :],
                      ot_sb[:, :], slot="o0")
                p.dwait("vector", "o0")
        p.dwait("gpsimd", "o0")
    return p.finalize()


def make_attn_masks():
    maskd = np.zeros((128, 4 * 512), np.float32)
    maskz = np.zeros((128, 4 * 512), np.float32)
    for j in range(4):
        for kp in range(128):
            for qf in range(512):
                qfp = qf - (j // 2) * 256
                if qfp < 0:
                    d, z = 0.0, 1.0
                elif qfp >= 256:
                    d, z = 1.0, 1.0
                else:
                    inc = 1.0 if (kp + (j % 2) * 128) <= qfp else 0.0
                    d, z = inc, inc
                maskd[kp, j * 512 + qf] = d
                maskz[kp, j * 512 + qf] = z
    return _bf16(maskd), _bf16(maskz)


def _bf16(x):
    import ml_dtypes
    return np.ascontiguousarray(x).astype(ml_dtypes.bfloat16)


def _ct_major(v, nct=NCT):
    return np.ascontiguousarray(v.reshape(nct, 128).T)


def prep_l0_mamba_core(g, c):
    ch0 = c * 512
    xr0 = c * 128
    T = lambda a: np.ascontiguousarray(a.T)
    sl = g
    m = {
        "wxT": T(sl["L0_attn_Wx"][xr0:xr0 + 128, :]),
        "wbT": T(sl["L0_attn_WB"][xr0:xr0 + 128, :]),
        "wzT": T(sl["L0_attn_Wz"][ch0:ch0 + 512, :]),
        "wcT": T(sl["L0_attn_WC"][ch0:ch0 + 512, :]),
        "wdtdT": T(sl["L0_attn_Wdt_down"]),
        "wdtT": T(sl["L0_attn_Wdt"][ch0:ch0 + 512, :]),
        "bx_s": np.ascontiguousarray(sl["L0_attn_bx"][xr0:xr0 + 128]
                                     .reshape(128, 1)),
        "bz_s": _ct_major(sl["L0_attn_bz"][ch0:ch0 + 512]),
        "bdt_s": _ct_major(sl["L0_attn_bdt"][ch0:ch0 + 512]),
        "d_s": _ct_major(sl["L0_attn_D"][ch0:ch0 + 512]),
        "convb_s": _ct_major(sl["L0_attn_conv_b"][ch0:ch0 + 512]),
        "alog_s": np.ascontiguousarray(
            sl["L0_attn_A_log"][ch0:ch0 + 512].reshape(NCT, 128, DS)
            .transpose(1, 0, 2).reshape(128, NCT * DS)),
        "convw_s": np.ascontiguousarray(
            sl["L0_attn_conv_w"][ch0:ch0 + 512, 0, :].reshape(NCT, 128, 4)
            .transpose(1, 0, 2).reshape(128, NCT * 4)),
    }
    selx = np.zeros((128, NCT * 128), np.float32)
    selB = np.zeros((128, NCT * DS * 128), np.float32)
    selC = np.zeros((128, DS * 128), np.float32)
    for ct in range(NCT):
        for cloc2 in range(128):
            c_loc = ct * 128 + cloc2
            xrow = (c_loc // 64) * 16 + (c_loc % 16)
            selx[xrow, ct * 128 + cloc2] = 1.0
            bh = c_loc // 64
            for s in range(DS):
                selB[bh * 16 + s, (ct * DS + s) * 128 + cloc2] = 1.0
    for s in range(DS):
        for cloc2 in range(128):
            selC[(cloc2 // 16) * 16 + s, s * 128 + cloc2] = 1.0
    m["selx"] = _bf16(selx)
    m["selB"] = _bf16(selB)
    m["selC"] = _bf16(selC)
    return m


def kernel(**inputs):
    g = inputs
    T = lambda a: np.ascontiguousarray(a.T)
    hs = np.ascontiguousarray(g["L0_attn_hidden_states"].reshape(NTOK, DM))
    tok_per = NTOK // NCORES
    cores = list(range(NCORES))

    nc0 = build_ln0(tok_per)
    maps0 = [{"hs": hs[c * tok_per:(c + 1) * tok_per]} for c in cores]
    r0 = run_bass_kernel_spmd(nc0, maps0, core_ids=cores)
    xn0_tm = np.concatenate([r0.results[c]["xn"] for c in cores], axis=0)

    nc1 = build_l0_mamba()
    maps1 = []
    for c in cores:
        m = prep_l0_mamba_core(g, c)
        m["xn"] = xn0_tm
        maps1.append(m)
    r1 = run_bass_kernel_spmd(nc1, maps1, core_ids=cores)
    yg_full = np.concatenate([r1.results[c]["yg"] for c in cores], axis=0)

    nc3 = build_l0_out(tok_per)
    woutT = T(g["L0_attn_Wout"])
    gT = T(g["L0_ffn_wq_gate"]); uT = T(g["L0_ffn_wq_up"])
    dT = T(g["L0_ffn_wq_down"])
    bout_s = np.ascontiguousarray(g["L0_attn_bout"].reshape(NDM, 128).T)
    sg_s = np.ascontiguousarray(g["L0_ffn_s_gate"].reshape(44, 128).T)
    su_s = np.ascontiguousarray(g["L0_ffn_s_up"].reshape(44, 128).T)
    sd_s = np.ascontiguousarray(g["L0_ffn_s_down"].reshape(NDM, 128).T)
    ident = np.eye(128, dtype=np.float32)
    maps3 = []
    for c in cores:
        t0 = c * tok_per
        maps3.append({
            "yg_own": np.ascontiguousarray(yg_full[:, t0:t0 + tok_per]),
            "xn0_own": xn0_tm[t0:t0 + tok_per],
            "hs_own": hs[t0:t0 + tok_per],
            "woutT": woutT, "bout_s": bout_s,
            "gT_i32": gT, "uT_i32": uT, "dT_i32": dT,
            "sg_s": sg_s, "su_s": su_s, "sd_s": sd_s, "ident": ident,
        })
    r3 = run_bass_kernel_spmd(nc3, maps3, core_ids=cores)
    x1_fm = np.concatenate([r3.results[c]["x1f"] for c in cores], axis=1)
    xn1_fm = np.concatenate([r3.results[c]["xn1f"] for c in cores], axis=1)

    nc4 = build_l1_attn()
    maskd, maskz = make_attn_masks()
    identb = _bf16(ident)
    maps4 = []
    for c in cores:
        r0_ = c * 256
        maps4.append({
            "xn1": xn1_fm,
            "wqT_s": T(g["L1_attn_Wq"][r0_:r0_ + 256, :]),
            "wkT_s": T(g["L1_attn_Wk"][r0_:r0_ + 256, :]),
            "wvT_s": T(g["L1_attn_Wv"][r0_:r0_ + 256, :]),
            "identb": identb, "maskd": maskd, "maskz": maskz,
        })
    r4 = run_bass_kernel_spmd(nc4, maps4, core_ids=cores)
    merged_fm = np.concatenate([r4.results[c]["merged"] for c in cores], axis=0)

    nc5 = build_l1_out(tok_per)
    woT = T(g["L1_attn_Wo"])
    w1T = T(g["L1_ffn_w1"]); w2T = T(g["L1_ffn_w2"]); w3T = T(g["L1_ffn_w3"])
    maps5 = []
    for c in cores:
        t0 = c * tok_per
        maps5.append({
            "mrg_own": np.ascontiguousarray(merged_fm[:, t0:t0 + tok_per]),
            "xn1_own": np.ascontiguousarray(xn1_fm[:, t0:t0 + tok_per]),
            "x1_own": np.ascontiguousarray(x1_fm[:, t0:t0 + tok_per]),
            "woT": woT, "w1T": w1T, "w2T": w2T, "w3T": w3T, "ident": ident,
        })
    r5 = run_bass_kernel_spmd(nc5, maps5, core_ids=cores)
    out = np.concatenate([r5.results[c]["final"] for c in cores], axis=0)
    return out.reshape(NBATCH, L, DM)
